# version 03 (final)
# baseline (speedup 1.0000x reference)
import contextlib
import numpy as np
import concourse.bass as bass
import concourse.mybir as mybir
from concourse.bass_utils import run_bass_kernel_spmd

F32 = mybir.dt.float32
BF16 = mybir.dt.bfloat16
I32 = mybir.dt.int32
AX = mybir.AxisListType
AF = mybir.ActivationFunctionType
ALU = mybir.AluOpType

T = 2048
D = 1024
NT = 16
NSL = 4
EPS_LN = 1e-5
GATE_NORM = 16.0
HF = 8
NE = 16
SE = 256
DF = 128
HID1 = 3072

ACTIVE = ("ffn0a", "att0", "ffn0b", "ffn1a", "att1", "ffn1b")
DBG = False
FFN1_LVL = 2
DBG_TILES = 0
AT0_SLICES = NSL
AT0_SCAN = True
AT0_LVL = 7
DEBUG_RACES = False

CONST_COLS = (1e-5, 1.0 / 1024.0, 1.0, -1.0, 1.0 / 128.0, 1e-6, 0.5, 2.0)

ENGINES = ("sync", "gpsimd", "tensor", "vector", "scalar")


class DStream:
    def __init__(self, nc, es, name, eng):
        self.sem = es.enter_context(nc.semaphore("d_" + name))
        self.val = 0
        self.eng = eng
        self.drained = 0


class Prog:
    def __init__(self, nc, es):
        self.nc = nc
        self.es = es
        self.q = {e: [] for e in ENGINES}
        self.sem = {e: es.enter_context(nc.semaphore("c_" + e)) for e in ENGINES}
        self.val = {e: 0 for e in ENGINES}
        self.streams = []

    def dstream(self, name, eng):
        s = DStream(self.nc, self.es, name, eng)
        self.streams.append(s)
        return s

    debug_serial = False
    serial_ve = True

    def op(self, eng, fn, tick=True, n=1):
        ser = (self.serial_ve and eng in ("vector", "scalar")) or \
              (self.debug_serial and eng == "tensor")
        if ser:
            prev = self.val[eng]
            if prev > 0:
                self.q[eng].append(
                    lambda e, s=self.sem[eng], v=prev: e.wait_ge(s, v))
            tick = True
        if tick:
            self.val[eng] += n
        sem = self.sem[eng] if tick else None
        def f(e, fn=fn, sem=sem, n=n):
            ins = fn(e)
            if sem is not None:
                ins.then_inc(sem, n)
        self.q[eng].append(f)
        return self.val[eng]

    def dma(self, stream, dst, src):
        stream.val += 16
        def f(e, d=dst, s=src, sem=stream.sem):
            e.dma_start(d, s).then_inc(sem, 16)
        self.q[stream.eng].append(f)
        return stream.val

    def dwait(self, eng, stream, val=None):
        v = stream.val if val is None else val
        assert v <= stream.val
        if v <= 0:
            return
        self.q[eng].append(lambda e, s=stream.sem, v=v: e.wait_ge(s, v))

    def wait(self, eng, src_eng, val=None):
        v = self.val[src_eng] if val is None else val
        assert v <= self.val[src_eng], "forward wait would deadlock"
        if v <= 0:
            return
        sem = self.sem[src_eng]
        self.q[eng].append(lambda e, s=sem, v=v: e.wait_ge(s, v))

    def mark(self, eng):
        return self.val[eng]

    def emit(self):
        for s in self.streams:
            if s.val > s.drained:
                self.q[s.eng].append(lambda e, sm=s.sem, v=s.val: e.wait_ge(sm, v))
                s.drained = s.val
        active = [e for e in ENGINES if self.q[e]]
        with self.nc.Block() as block:
            for e in active:
                lst = list(self.q[e])
                def body(eng, lst=lst):
                    for f in lst:
                        f(eng)
                getattr(block, e)(body)
        self.q = {e: [] for e in ENGINES}


def mm_group(P, out, passes, first, last, tp=None, tick=True):
    n = len(passes)
    r = None
    for i, (lh, rh) in enumerate(passes):
        st = first and i == 0
        sp = last and i == n - 1
        r = P.op("tensor",
                 lambda e, o=out, a=lh, b=rh, st=st, sp=sp, tp=tp:
                 e.matmul(o, a, b, start=st, stop=sp, tile_position=tp),
                 tick=(tick and sp))
    return r


def p3(lh, ll, rh, rl):
    return [(lh, rh), (lh, rl), (ll, rh)]


def build_nc():
    nc = bass.Bass("TRN2", target_bir_lowering=False,
                   detect_race_conditions=DEBUG_RACES)
    es = contextlib.ExitStack()

    def din(name, shape, dt=F32):
        return nc.dram_tensor(name, shape, dt, kind="ExternalInput").ap()

    def dscr(name, shape, dt=BF16):
        return nc.dram_tensor(name, shape, dt, kind="Internal").ap()

    x_in = din("x_in", [T, D])
    a0 = {w: din("a0_" + w, s) for w, s in [
        ("wq", [D, 512]), ("wk", [D, 512]), ("wv", [D, D]), ("wg", [D, D]),
        ("wgk1", [D, 16]), ("wgk2", [16, 512]), ("wo", [D, D]),
        ("bgk2", [1, 512]), ("nw", [1, 128])]}
    f0 = {w: din("f0_" + w, s) for w, s in [
        ("rt", [D, 16]), ("mh", [D, D]), ("mg", [D, D]),
        ("kT", [HF, 128, NE * SE]), ("vi", [HF, 128, NE * SE])]}
    a1 = {w: din("a1_" + w, s) for w, s in [
        ("wq", [D, D]), ("wk", [D, D]), ("wv", [D, D]), ("wg", [D, D]),
        ("wo", [D, D]), ("gam", [1, D]), ("bet", [1, D])]}
    f1 = {w: din("f1_" + w, s, I32) for w, s in [
        ("wg", [D, HID1]), ("wu", [D, HID1]), ("wd", [HID1, D])]}
    cst = {w: din("c_" + w, s, dt) for w, s, dt in [
        ("ident", [128, 128], F32), ("tmbd", [128, 128], BF16),
        ("mask2", [128, 128], F32), ("ones1", [1, 128], F32),
        ("onesc", [128, 1], F32), ("eye64d", [128, 64], F32),
        ("cols", [128, len(CONST_COLS)], F32)]}
    out = nc.dram_tensor("out", [T, D], F32, kind="ExternalOutput").ap()
    dbg = nc.dram_tensor("dbg", [128, 4 * D], F32, kind="ExternalOutput").ap()
    dbgx = nc.dram_tensor("dbgx", [128, 4096], BF16, kind="ExternalOutput").ap()
    dbgq = nc.dram_tensor("dbgq", [128, 4096], BF16, kind="ExternalOutput").ap()
    dbgw = nc.dram_tensor("dbgw", [128, 64], F32, kind="ExternalOutput").ap()
    dbgl = nc.dram_tensor("dbgl", [128, 64], F32, kind="ExternalOutput").ap()

    scr = {}
    fp32_w = {
        "a0_wq": (a0["wq"], [D, 512]), "a0_wk": (a0["wk"], [D, 512]),
        "a0_wv": (a0["wv"], [D, D]), "a0_wg": (a0["wg"], [D, D]),
        "a0_wgk1": (a0["wgk1"], [D, 16]), "a0_wgk2": (a0["wgk2"], [16, 512]),
        "a0_wo": (a0["wo"], [D, D]),
        "f0_rt": (f0["rt"], [D, 16]), "f0_mh": (f0["mh"], [D, D]),
        "f0_mg": (f0["mg"], [D, D]),
        "f0_kT": (f0["kT"].rearrange("h p m -> (h p) m"), [HF * 128, NE * SE]),
        "f0_vi": (f0["vi"].rearrange("h p m -> (h p) m"), [HF * 128, NE * SE]),
        "a1_wq": (a1["wq"], [D, D]), "a1_wk": (a1["wk"], [D, D]),
        "a1_wv": (a1["wv"], [D, D]), "a1_wg": (a1["wg"], [D, D]),
        "a1_wo": (a1["wo"], [D, D]),
    }
    for nm, (ap, sh) in fp32_w.items():
        scr[nm + "_h"] = dscr("s_" + nm + "_h", sh)
        scr[nm + "_l"] = dscr("s_" + nm + "_l", sh)
    for nm, src in (("f1_wg", f1["wg"]), ("f1_wu", f1["wu"]), ("f1_wd", f1["wd"])):
        scr[nm] = dscr("s_" + nm, list(src.shape))

    sb = nc.alloc_sbuf_tensor
    x_res = sb("x_res", [128, NT * D], F32)
    xlt_h = sb("xlt_h", [128, 8 * 512], BF16)
    xlt_l = sb("xlt_l", [128, 8 * 512], BF16)
    ln_nm = sb("ln_nm", [128, D], F32)
    ln_s1 = sb("ln_s1", [128, 1], F32)
    ln_mu = sb("ln_mu", [128, 1], F32)
    ln_ssq = sb("ln_ssq", [128, 1], F32)
    ln_va = sb("ln_va", [128, 1], F32)
    ln_m2 = sb("ln_m2", [128, 1], F32)
    ln_sd = sb("ln_sd", [128, 1], F32)
    ln_rs = sb("ln_rs", [128, 1], F32)
    idn = sb("idn", [128, 128], F32)
    tmbd = sb("tmbd", [128, 128], BF16)
    mask2w = sb("mask2w", [128, 128], F32)
    ones1 = sb("ones1", [1, 128], F32)
    onesc = sb("onesc", [128, 1], F32)
    eye64d = sb("eye64d", [128, 64], F32)
    wbh = [sb(f"wbh{i}", [128, 4096], BF16) for i in range(2)]
    wbl = [sb(f"wbl{i}", [128, 4096], BF16) for i in range(2)]
    ccol = sb("ccol", [128, len(CONST_COLS)], F32)
    for i, v_ in enumerate(CONST_COLS):
        nc.const_aps.register(F32, v_, ccol[:, i:i + 1])
    pb = [nc.alloc_psum_tensor(f"pb{i}", [128, 512], F32) for i in range(8)]

    P = Prog(nc, es)
    P.debug_serial = False
    P._ln_tr = 0
    st_ld0 = P.dstream("ld0", "sync")
    st_bc = P.dstream("bc", "sync")
    st_pin = [P.dstream(f"pin{i}", "sync") for i in range(2)]
    st_pout = [P.dstream(f"pout{i}", "gpsimd") for i in range(2)]
    st_ic = [P.dstream(f"ic{i}", "gpsimd") for i in range(2)]
    st_ist = [P.dstream(f"ist{i}", "sync") for i in range(2)]
    st_w = [P.dstream(f"w{i}", "sync") for i in range(2)]
    st_out = P.dstream("ost", "gpsimd")

    p0es = contextlib.ExitStack()
    pw32 = [p0es.enter_context(nc.sbuf_tensor(f"pw32_{i}", [128, 4096], F32)) for i in range(2)]
    pwh = [p0es.enter_context(nc.sbuf_tensor(f"pwh{i}", [128, 4096], BF16)) for i in range(2)]
    pwl = [p0es.enter_context(nc.sbuf_tensor(f"pwl{i}", [128, 4096], BF16)) for i in range(2)]
    P.dma(st_ld0, x_res[:].rearrange("p (n f) -> p n f", n=NT),
          x_in.rearrange("(n p) f -> p n f", p=128))
    for nm, t_ in [("ident", idn), ("tmbd", tmbd), ("mask2", mask2w),
                   ("ones1", ones1), ("onesc", onesc), ("eye64d", eye64d),
                   ("cols", ccol)]:
        P.dma(st_ld0, t_[:], cst[nm])
    def build_bcast(src_ap, dst, width):
        P.wait("sync", "tensor", P.mark("tensor"))
        ld = P.dma(st_bc, ln_nm[0:1, 0:width], src_ap)
        P.dwait("tensor", st_bc, ld)
        P.wait("tensor", "vector", P.mark("vector"))
        nb = (width + 511) // 512
        for j in range(nb):
            c0, c1 = j * 512, min(width, (j + 1) * 512)
            P.op("tensor", lambda e, j=j, c0=c0, c1=c1:
                 e.matmul(pb[j][:, 0:c1 - c0], ones1[:], ln_nm[0:1, c0:c1], start=True, stop=True))
        mv = P.mark("tensor")
        P._ln_tr = mv
        P.wait("vector", "tensor", mv)
        for j in range(nb):
            c0, c1 = j * 512, min(width, (j + 1) * 512)
            P.op("vector", lambda e, j=j, c0=c0, c1=c1, dst=dst:
                 e.tensor_copy(dst[:, c0:c1], pb[j][:, 0:c1 - c0]))

    chunks = []
    for nm, (ap, sh) in fp32_w.items():
        R, C = sh
        r = 0
        while r < R:
            rr = min(128, R - r)
            chunks.append((ap[r:r + rr, :], scr[nm + "_h"][r:r + rr, :],
                           scr[nm + "_l"][r:r + rr, :], rr, C))
            r += rr
    st_mark = [0] * len(chunks)
    for i, (src, dh, dl, rr, cc) in enumerate(chunks):
        bi = i % 2
        if i >= 2:
            P.dwait("sync", st_pout[bi], st_mark[i - 2])
        ldv = P.dma(st_pin[bi], pw32[bi][0:rr, 0:cc], src)
        P.dwait("vector", st_pin[bi], ldv)
        if i >= 2:
            P.dwait("vector", st_pout[bi], st_mark[i - 2])
        P.op("vector", lambda e, bi=bi, rr=rr, cc=cc:
             e.tensor_copy(pwh[bi][0:rr, 0:cc], pw32[bi][0:rr, 0:cc]), tick=False)
        spv = P.op("vector", lambda e, bi=bi, rr=rr, cc=cc:
                   e.tensor_sub(pwl[bi][0:rr, 0:cc], pw32[bi][0:rr, 0:cc], pwh[bi][0:rr, 0:cc]))
        P.wait("gpsimd", "vector", spv)
        P.dma(st_pout[bi], dh, pwh[bi][0:rr, 0:cc])
        st_mark[i] = P.dma(st_pout[bi], dl, pwl[bi][0:rr, 0:cc])
    ichunks = []
    for nm in ("f1_wg", "f1_wu", "f1_wd"):
        src = f1[nm.split("_")[1]]
        R, C = src.shape
        for r in range(0, R, 128):
            ichunks.append((src[r:r + 128, :], scr[nm][r:r + 128, :], C))
    ist = [0] * len(ichunks)
    for i, (src, dst, cc) in enumerate(ichunks):
        bi = i % 2
        if i >= 2:
            P.dwait("gpsimd", st_ist[bi], ist[i - 2])
        cv = P.dma(st_ic[bi], wbh[bi][:, 0:cc], src)
        P.dwait("sync", st_ic[bi], cv)
        ist[i] = P.dma(st_ist[bi], dst, wbh[bi][:, 0:cc])
    P.emit()
    p0es.close()

    def ln_tile(ts, tb0, tb1):
        s = ts % 4
        xr = x_res[:, ts * D:(ts + 1) * D]
        sv = P.op("vector", lambda e, xr=xr: e.tensor_reduce(ln_s1[:], xr, AX.X, ALU.add))
        P.wait("scalar", "tensor", P._ln_tr)
        a = P.op("scalar", lambda e, xr=xr: e.activation(ln_nm[:], xr, AF.Square, accum_out=ln_ssq[:]))
        P.wait("vector", "vector", sv)
        v = P.op("vector", lambda e: e.tensor_single_scalar(ln_mu[:], ln_s1[:], 1.0 / D, ALU.mult))
        P.wait("vector", "scalar", a)
        P.op("vector", lambda e: e.tensor_single_scalar(ln_va[:], ln_ssq[:], 1.0 / D, ALU.mult), tick=False)
        P.wait("vector", "vector", v)
        P.op("vector", lambda e: e.tensor_mul(ln_m2[:], ln_mu[:], ln_mu[:]), tick=False)
        v = P.op("vector", lambda e: e.tensor_sub(ln_va[:], ln_va[:], ln_m2[:]))
        P.wait("scalar", "vector", v)
        a = P.op("scalar", lambda e: e.activation(ln_sd[:], ln_va[:], AF.Sqrt, bias=EPS_LN))
        P.wait("vector", "scalar", a)
        v = P.op("vector", lambda e: e.reciprocal(ln_rs[:], ln_sd[:]))
        P.wait("vector", "vector", v)
        P.wait("vector", "tensor", P._ln_tr)
        v = P.op("vector", lambda e, xr=xr:
                 e.tensor_scalar(ln_nm[:], xr, ln_mu[:], ln_rs[:], ALU.subtract, ALU.mult))
        _transpose_split(ln_nm, xlt_h, xlt_l, s, tb0, tb1, v)

    def _transpose_split(src32, dsth, dstl, s, tb0, tb1, dve_ready):
        P.wait("tensor", "vector", dve_ready)
        for g, bank in ((0, tb0), (1, tb1)):
            for j in range(4):
                kc = 4 * g + j
                P.op("tensor", lambda e, b=bank, j=j, kc=kc, src32=src32:
                     e.matmul(pb[b][:, j * 128:(j + 1) * 128], src32[:, kc * 128:(kc + 1) * 128],
                              idn[:], is_transpose=True, start=(j == 0), stop=(j == 3)),
                     tick=(j == 3))
        P._ln_tr = P.mark("tensor")
        P.wait("vector", "tensor", P._ln_tr)
        for g, bank in ((0, tb0), (1, tb1)):
            hv = dsth[:].rearrange("p (kc t) -> p kc t", kc=8)[:, 4 * g:4 * g + 4, s * 128:(s + 1) * 128]
            lv = dstl[:].rearrange("p (kc t) -> p kc t", kc=8)[:, 4 * g:4 * g + 4, s * 128:(s + 1) * 128]
            pv = pb[bank][:].rearrange("p (j t) -> p j t", j=4)
            P.op("vector", lambda e, hv=hv, pv=pv: e.tensor_copy(hv, pv), tick=False)
            P.op("vector", lambda e, lv=lv, pv=pv, hv=hv: e.tensor_tensor(lv, pv, hv, ALU.subtract))
        return P.mark("vector")

    def tr_split(src32, dsth, dstl, s, tb0, tb1):
        return _transpose_split(src32, dsth, dstl, s, tb0, tb1, P.mark("vector"))

    def stream_nm(nm, buf, m0, width, prev_pe):
        P.wait("sync", "tensor", prev_pe)
        src_h = scr[nm + "_h"].rearrange("(kc p) m -> p kc m", p=128)[:, :, m0:m0 + width]
        src_l = scr[nm + "_l"].rearrange("(kc p) m -> p kc m", p=128)[:, :, m0:m0 + width]
        P.dma(st_w[buf], wbh[buf][:, 0:8 * width].rearrange("p (kc m) -> p kc m", kc=8), src_h)
        return P.dma(st_w[buf], wbl[buf][:, 0:8 * width].rearrange("p (kc m) -> p kc m", kc=8), src_l)

    def emit_ffn0(tag):
        ph = contextlib.ExitStack()
        sbt = lambda n, s, d: ph.enter_context(nc.sbuf_tensor(tag + n, s, d))
        qh = sbt("qh", [128, 8 * 512], BF16)
        ql = sbt("ql", [128, 8 * 512], BF16)
        wgt = sbt("wgt", [128, 4 * 16], F32)
        acth = sbt("acth", [128, 2 * 512], BF16)
        actl = sbt("actl", [128, 2 * 512], BF16)
        y_acc = sbt("y_acc", [128, 4 * D], F32)
        ysh, ysl = xlt_h, xlt_l
        lgt = sbt("lgt", [128, 16], F32)
        lgts = sbt("lgts", [128, 64], F32)
        ltm = sbt("ltm", [128, 16], F32)
        m1 = sbt("m1", [128, 1], F32)
        m2 = sbt("m2", [128, 1], F32)
        eqm = sbt("eqm", [128, 16], F32)
        sgm = sbt("sgm", [128, 16], F32)
        rth = sbt("rth", [128, 8 * 16], BF16)
        rtl = sbt("rtl", [128, 8 * 16], BF16)
        kkh = [sbt(f"kkh{i}", [128, 4096], BF16) for i in range(2)]
        kkl = [sbt(f"kkl{i}", [128, 4096], BF16) for i in range(2)]
        vvh = sbt("vvh", [128, 4096], BF16)
        vvl = sbt("vvl", [128, 4096], BF16)

        st_rt = P.dstream(tag + "rt", "sync")
        st_k = [P.dstream(tag + f"k{i}", "sync") for i in range(2)]
        st_v = P.dstream(tag + "v", "sync")
        P.dma(st_rt, rth[:].rearrange("p (kc m) -> p kc m", kc=8),
              scr["f0_rt_h"].rearrange("(kc p) m -> p kc m", p=128))
        rt_v = P.dma(st_rt, rtl[:].rearrange("p (kc m) -> p kc m", kc=8),
                     scr["f0_rt_l"].rearrange("(kc p) m -> p kc m", p=128))

        def load_k(h, buf):
            P.dma(st_k[buf], kkh[buf][:], scr["f0_kT_h"][h * 128:(h + 1) * 128, :])
            return P.dma(st_k[buf], kkl[buf][:], scr["f0_kT_l"][h * 128:(h + 1) * 128, :])

        def load_v(h):
            P.dma(st_v, vvh[:], scr["f0_vi_h"][h * 128:(h + 1) * 128, :])
            return P.dma(st_v, vvl[:], scr["f0_vi_l"][h * 128:(h + 1) * 128, :])

        k_ld = {0: load_k(0, 0)}
        v_ld = {0: load_v(0)}
        k_free = {0: 0, 1: 0}
        sc_m = {0: 0, 1: 0}
        ge_m = {0: 0, 1: 0}
        sp_m = {0: 0, 1: 0}
        yv_m = [0, 0, 0, 0]
        rt_mm = [0, 0]
        rt_dve = [0, 0]

        for t in range(4):
            for s in range(4):
                ts = 4 * t + s
                ln_tile(ts, 4, 5)
                xlt_v = P.mark("vector")
                bank = 6 + (s % 2)
                P.wait("tensor", "vector", xlt_v)
                if t == 0 and s == 0:
                    P.dwait("tensor", st_rt, rt_v)
                P.wait("tensor", "vector", rt_dve[s % 2])
                passes = []
                for kc in range(8):
                    xh = xlt_h[:, kc * 512 + s * 128: kc * 512 + (s + 1) * 128]
                    xl = xlt_l[:, kc * 512 + s * 128: kc * 512 + (s + 1) * 128]
                    passes += p3(xh, xl, rth[:, kc * 16:(kc + 1) * 16], rtl[:, kc * 16:(kc + 1) * 16])
                mmv = mm_group(P, pb[bank][:, 0:16], passes, True, True)
                P.wait("vector", "tensor", mmv)
                rt_dve[s % 2] = P.op("vector", lambda e, bank=bank: e.tensor_copy(lgt[:], pb[bank][:, 0:16]))
                if DBG:
                    P.op("vector", lambda e, s=s: e.tensor_copy(lgts[:, s * 16:(s + 1) * 16], lgt[:]), tick=False)
                v = P.op("vector", lambda e: e.reduce_max(m1[:], lgt[:], AX.X))
                P.wait("vector", "vector", v)
                P.op("vector", lambda e: e.tensor_scalar(eqm[:], lgt[:], m1[:], None, ALU.is_equal), tick=False)
                v = P.op("vector", lambda e: e.scalar_tensor_tensor(ltm[:], eqm[:], -1e30, lgt[:], ALU.mult, ALU.add))
                P.wait("vector", "vector", v)
                v = P.op("vector", lambda e: e.reduce_max(m2[:], ltm[:], AX.X))
                P.wait("vector", "vector", v)
                P.op("vector", lambda e: e.tensor_scalar(eqm[:], lgt[:], m2[:], None, ALU.is_ge), tick=False)
                P.wait("scalar", "vector", rt_dve[s % 2])
                sgv = P.op("scalar", lambda e: e.activation(sgm[:], lgt[:], AF.Sigmoid))
                P.wait("vector", "scalar", sgv)
                P.op("vector", lambda e, s=s: e.tensor_mul(wgt[:, s * 16:(s + 1) * 16], sgm[:], eqm[:]))
            gat_v = P.mark("vector")
            if t == 0 and DBG:
                P.wait("gpsimd", "vector", gat_v)
                P.dma(st_out, dbgx, xlt_h[:])
                P.dma(st_out, dbgw, wgt[:])
                P.dma(st_out, dbgl, lgts[:])
            qmm = [0] * 8
            qsp = [0] * 8
            for m in range(8):
                buf = m % 2
                P.wait("sync", "tensor", qmm[m - 2] if m >= 2 else P._ln_tr)
                src_h = scr["f0_mh_h"].rearrange("(kc p) m -> p kc m", p=128)[:, :, m * 128:(m + 1) * 128]
                src_l = scr["f0_mh_l"].rearrange("(kc p) m -> p kc m", p=128)[:, :, m * 128:(m + 1) * 128]
                P.dma(st_w[buf], wbh[buf][:, 0:1024].rearrange("p (kc m) -> p kc m", kc=8), src_h)
                wv = P.dma(st_w[buf], wbl[buf][:, 0:1024].rearrange("p (kc m) -> p kc m", kc=8), src_l)
                bank = 6 + (m % 2)
                P.dwait("tensor", st_w[buf], wv)
                P.wait("tensor", "vector", gat_v if m < 2 else qsp[m - 2])
                passes = []
                for kc in range(8):
                    wh_ = wbh[buf][:, kc * 128:(kc + 1) * 128]
                    wl_ = wbl[buf][:, kc * 128:(kc + 1) * 128]
                    xh = xlt_h[:, kc * 512:(kc + 1) * 512]
                    xl = xlt_l[:, kc * 512:(kc + 1) * 512]
                    passes += p3(wh_, wl_, xh, xl)
                qmm[m] = mm_group(P, pb[bank][:, 0:512], passes, True, True)
                P.wait("vector", "tensor", qmm[m])
                P.op("vector", lambda e, m=m, bank=bank:
                     e.tensor_copy(qh[:, m * 512:(m + 1) * 512], pb[bank][:, 0:512]), tick=False)
                qsp[m] = P.op("vector", lambda e, m=m, bank=bank:
                              e.tensor_tensor(ql[:, m * 512:(m + 1) * 512], pb[bank][:, 0:512],
                                              qh[:, m * 512:(m + 1) * 512], ALU.subtract))
            for s in range(4):
                P.op("vector", lambda e, s=s: e.memset(y_acc[:, s * D:(s + 1) * D], 0.0),
                     tick=(s == 3))
            yz_v = P.mark("vector")
            if t == 0 and DBG:
                P.wait("gpsimd", "vector", P.mark("vector"))
                P.dma(st_out, dbgq, qh[:])

            for h in range(HF):
                gi = t * HF + h
                buf = gi % 2
                if gi + 1 < 4 * HF:
                    nbuf = (gi + 1) % 2
                    P.wait("sync", "tensor", k_free[nbuf])
                    k_ld[gi + 1] = load_k((h + 1) % HF, nbuf)
                P.dwait("tensor", st_k[buf], k_ld[gi])
                P.dwait("tensor", st_v, v_ld[gi])
                if h == 0:
                    P.wait("tensor", "vector", yz_v)
                for c in range(32):
                    par = c % 2
                    e_idx = c // 2
                    P.wait("tensor", "scalar", ge_m[par])
                    kh_ = kkh[buf][:, c * 128:(c + 1) * 128]
                    kl_ = kkl[buf][:, c * 128:(c + 1) * 128]
                    qh_ = qh[:, h * 512:(h + 1) * 512]
                    ql_ = ql[:, h * 512:(h + 1) * 512]
                    sc_m[par] = mm_group(P, pb[par][:, 0:512], p3(kh_, kl_, qh_, ql_), True, True)
                    P.wait("scalar", "tensor", sc_m[par])
                    P.wait("scalar", "vector", sp_m[par])
                    ge_m[par] = P.op("scalar", lambda e, par=par:
                                     e.activation(pb[2 + par][:, 0:512], pb[par][:, 0:512], AF.Gelu))
                    P.wait("vector", "scalar", ge_m[par])
                    P.op("vector", lambda e, par=par:
                         e.tensor_copy(acth[:, par * 512:(par + 1) * 512], pb[2 + par][:, 0:512]),
                         tick=False)
                    sp_m[par] = P.op("vector", lambda e, par=par:
                                     e.tensor_tensor(actl[:, par * 512:(par + 1) * 512],
                                                     pb[2 + par][:, 0:512],
                                                     acth[:, par * 512:(par + 1) * 512], ALU.subtract))
                    P.wait("tensor", "vector", sp_m[par])
                    for s in range(4):
                        if par == 0:
                            P.wait("tensor", "vector", yv_m[s])
                        ah_ = acth[:, par * 512 + s * 128: par * 512 + (s + 1) * 128]
                        al_ = actl[:, par * 512 + s * 128: par * 512 + (s + 1) * 128]
                        vh_ = vvh[:, c * 128:(c + 1) * 128]
                        vl_ = vvl[:, c * 128:(c + 1) * 128]
                        yv_m[s] = mm_group(P, pb[4 + s][:, 0:128], p3(ah_, al_, vh_, vl_),
                                           first=(par == 0), last=(par == 1))
                    if par == 1:
                        for s in range(4):
                            P.wait("vector", "tensor", yv_m[s])
                            yv_m[s] = P.op("vector", lambda e, s=s, h=h, e_idx=e_idx:
                                           e.scalar_tensor_tensor(
                                               y_acc[:, s * D + h * 128: s * D + (h + 1) * 128],
                                               pb[4 + s][:, 0:128],
                                               wgt[:, s * 16 + e_idx: s * 16 + e_idx + 1],
                                               y_acc[:, s * D + h * 128: s * D + (h + 1) * 128],
                                               ALU.mult, ALU.add))
                k_free[buf] = P.mark("tensor")
                if gi + 1 < 4 * HF:
                    P.wait("sync", "tensor", P.mark("tensor"))
                    v_ld[gi + 1] = load_v((h + 1) % HF)
            if t == 0 and DBG:
                P.wait("gpsimd", "vector", P.mark("vector"))
                P.dma(st_out, dbg, y_acc[:])
                P.dwait("vector", st_out)
            if t == 0 and DBG_TILES == 1:
                break
            for s in range(4):
                tr_split(y_acc[:, s * D:(s + 1) * D], ysh, ysl, s, 4, 5)
            ytr_v = P.mark("vector")
            fin_v = [0, 0]
            for n in range(2):
                buf = n % 2
                P.wait("sync", "tensor", P._ln_tr)
                src_h = scr["f0_mg_h"].rearrange("(kc p) m -> p kc m", p=128)[:, :, n * 512:(n + 1) * 512]
                src_l = scr["f0_mg_l"].rearrange("(kc p) m -> p kc m", p=128)[:, :, n * 512:(n + 1) * 512]
                P.dma(st_w[buf], wbh[buf][:, 0:4096].rearrange("p (kc m) -> p kc m", kc=8), src_h)
                wv = P.dma(st_w[buf], wbl[buf][:, 0:4096].rearrange("p (kc m) -> p kc m", kc=8), src_l)
                P.dwait("tensor", st_w[buf], wv)
                P.wait("tensor", "vector", ytr_v)
                for s in range(4):
                    bank = 6 + (s % 2)
                    P.wait("tensor", "vector", fin_v[s % 2])
                    passes = []
                    for kc in range(8):
                        yh_ = ysh[:, kc * 512 + s * 128: kc * 512 + (s + 1) * 128]
                        yl_ = ysl[:, kc * 512 + s * 128: kc * 512 + (s + 1) * 128]
                        wh_ = wbh[buf][:, kc * 512:(kc + 1) * 512]
                        wl_ = wbl[buf][:, kc * 512:(kc + 1) * 512]
                        passes += p3(yh_, yl_, wh_, wl_)
                    fv = mm_group(P, pb[bank][:, 0:512], passes, True, True)
                    P.wait("vector", "tensor", fv)
                    ts = 4 * t + s
                    fin_v[s % 2] = P.op("vector", lambda e, bank=bank, ts=ts, n=n:
                                        e.scalar_tensor_tensor(
                                            x_res[:, ts * D + n * 512: ts * D + (n + 1) * 512],
                                            pb[bank][:, 0:512], 0.5,
                                            x_res[:, ts * D + n * 512: ts * D + (n + 1) * 512],
                                            ALU.mult, ALU.add))
        P.emit()
        ph.close()

    def emit_att0():
        ph = contextlib.ExitStack()
        sbt = lambda n, s, d: ph.enter_context(nc.sbuf_tensor("at0" + n, s, d))
        qf = sbt("qf", [128, 4 * 512], F32)
        kf = sbt("kf", [128, 4 * 512], F32)
        sph = sbt("sph", [128, 4 * 512], BF16)
        spl = sbt("spl", [128, 4 * 512], BF16)
        vth = sbt("vth", [128, 4 * 1024], BF16)
        vtl = sbt("vtl", [128, 4 * 1024], BF16)
        sg0 = sbt("sg0", [128, 4 * 1024], F32)
        ogh, ogl = xlt_h, xlt_l
        s32 = sbt("s32", [128, 4 * 128], F32)
        s16h = sbt("s16h", [128, 4 * 128], BF16)
        s16l = sbt("s16l", [128, 4 * 128], BF16)
        eb = sbt("eb", [128, 128], F32)
        enb = sbt("enb", [128, 128], F32)
        edc = sbt("edc", [128, 128], F32)
        blc = sbt("blc", [128, 2], F32)
        elc = sbt("elc", [128, 2], F32)
        qe32 = sbt("qe32", [128, 128], F32)
        ke32 = sbt("ke32", [128, 128], F32)
        kd32 = sbt("kd32", [128, 128], F32)
        qeh = sbt("qeh", [128, 128], BF16)
        qel = sbt("qel", [128, 128], BF16)
        keh = sbt("keh", [128, 128], BF16)
        kel = sbt("kel", [128, 128], BF16)
        kdph = sbt("kdph", [128, 128], BF16)
        kdpl = sbt("kdpl", [128, 128], BF16)
        ath = sbt("ath", [128, 128], BF16)
        atl = sbt("atl", [128, 128], BF16)
        osq = sbt("osq", [128, 2], F32)
        osd = sbt("osd", [128, 2], F32)
        orc = sbt("orc", [128, 2], F32)
        oscr = sbt("oscr", [128, 256], F32)
        og32 = sbt("og32", [128, 256], F32)
        u_sb = sbt("u_sb", [128, 4 * 16], F32)
        uTh = sbt("uTh", [16, 512], BF16)
        uTl = sbt("uTl", [16, 512], BF16)
        w2h = sbt("w2h", [16, 512], BF16)
        w2l = sbt("w2l", [16, 512], BF16)
        g1h = sbt("g1h", [128, 8 * 16], BF16)
        g1l = sbt("g1l", [128, 8 * 16], BF16)
        bgk_bc = sbt("bgkbc", [128, 512], F32)
        nw_bc = sbt("nwbc", [128, 128], F32)
        vscr = ln_nm[:, 0:512]
        sp32 = ln_nm[:, 512:1024]

        build_bcast(a0["bgk2"], bgk_bc, 512)
        build_bcast(a0["nw"], nw_bc, 128)
        if DBG:
            P.dma(st_out, dbgw.rearrange("p m -> (p m)").rearrange("(a b) -> a b", a=64)[0:64, :]
                  if False else dbgw, scr["a0_wq_h"][0:128, 0:64])
            P.dma(st_out, dbgl, scr["a0_wq_l"][0:128, 0:64])
        st_sm = P.dstream("at0sm", "sync")
        P.dma(st_sm, w2h[:], scr["a0_wgk2_h"])
        P.dma(st_sm, w2l[:], scr["a0_wgk2_l"])
        P.dma(st_sm, g1h[:].rearrange("p (kc m) -> p kc m", kc=8),
              scr["a0_wgk1_h"].rearrange("(kc p) m -> p kc m", p=128))
        sm_v = P.dma(st_sm, g1l[:].rearrange("p (kc m) -> p kc m", kc=8),
                     scr["a0_wgk1_l"].rearrange("(kc p) m -> p kc m", p=128))

        def proj_f(nm, nm_m, dst32):
            pj = [0] * nm_m
            for m in range(nm_m):
                buf = m % 2
                wv = stream_nm(nm, buf, m * 128, 128, pj[m - 2] if m >= 2 else P.mark("tensor"))
                bank = 4 + (m % 2)
                P.dwait("tensor", st_w[buf], wv)
                P.wait("tensor", "vector", P.mark("vector"))
                passes = []
                for kc in range(8):
                    passes += p3(wbh[buf][:, kc * 128:(kc + 1) * 128],
                                 wbl[buf][:, kc * 128:(kc + 1) * 128],
                                 xlt_h[:, kc * 512:(kc + 1) * 512],
                                 xlt_l[:, kc * 512:(kc + 1) * 512])
                pj[m] = mm_group(P, pb[bank][:, 0:512], passes, True, True)
                P.wait("vector", "tensor", pj[m])
                P.op("vector", lambda e, m=m, bank=bank:
                     e.tensor_copy(dst32[:, m * 512:(m + 1) * 512], pb[bank][:, 0:512]))
            return P.mark("vector")

        def proj_t(nm, dsth, dstl, dst32, silu):
            pj = [0] * 8
            pev = [0] * 8
            for idx in range(8):
                s, n = idx // 2, idx % 2
                buf = idx % 2
                wv = stream_nm(nm, buf, n * 512, 512, pj[idx - 2] if idx >= 2 else P.mark("tensor"))
                bank = 4 + (idx % 2)
                P.dwait("tensor", st_w[buf], wv)
                if idx >= 2:
                    P.wait("tensor", "vector" if not silu else "scalar", pev[idx - 2])
                passes = []
                for kc in range(8):
                    passes += p3(xlt_h[:, kc * 512 + s * 128: kc * 512 + (s + 1) * 128],
                                 xlt_l[:, kc * 512 + s * 128: kc * 512 + (s + 1) * 128],
                                 wbh[buf][:, kc * 512:(kc + 1) * 512],
                                 wbl[buf][:, kc * 512:(kc + 1) * 512])
                pj[idx] = mm_group(P, pb[bank][:, 0:512], passes, True, True)
                col = s * 1024 + n * 512
                if silu:
                    P.wait("scalar", "tensor", pj[idx])
                    pev[idx] = P.op("scalar", lambda e, bank=bank, col=col:
                                    e.activation(dst32[:, col:col + 512], pb[bank][:, 0:512], AF.Silu))
                else:
                    P.wait("vector", "tensor", pj[idx])
                    P.op("vector", lambda e, bank=bank, col=col:
                         e.tensor_copy(dsth[:, col:col + 512], pb[bank][:, 0:512]))
                    pev[idx] = P.op("vector", lambda e, bank=bank, col=col:
                                    e.tensor_tensor(dstl[:, col:col + 512], pb[bank][:, 0:512],
                                                    dsth[:, col:col + 512], ALU.subtract))
            return max(P.mark("vector"), P.mark("scalar"))

        for sl in range(AT0_SLICES):
            for s in range(4):
                ln_tile(4 * sl + s, 6, 7)
            proj_f("a0_wq", 4, qf)
            proj_f("a0_wk", 4, kf)
            proj_t("a0_wv", vth, vtl, None, False)
            proj_t("a0_wg", None, None, sg0, True)
            P.dwait("tensor", st_sm, sm_v)
            for s in range(4):
                P.wait("tensor", "vector", P.mark("vector"))
                passes = []
                for kc in range(8):
                    passes += p3(xlt_h[:, kc * 512 + s * 128: kc * 512 + (s + 1) * 128],
                                 xlt_l[:, kc * 512 + s * 128: kc * 512 + (s + 1) * 128],
                                 g1h[:, kc * 16:(kc + 1) * 16],
                                 g1l[:, kc * 16:(kc + 1) * 16])
                uv = mm_group(P, pb[3][:, 0:16], passes, True, True)
                P.wait("vector", "tensor", uv)
                P.op("vector", lambda e, s=s: e.tensor_copy(u_sb[:, s * 16:(s + 1) * 16], pb[3][:, 0:16]))
            uev = P.mark("vector")
            P.wait("tensor", "vector", uev)
            for s in range(4):
                P.op("tensor", lambda e, s=s:
                     e.matmul(pb[2][0:16, s * 128:(s + 1) * 128], u_sb[:, s * 16:(s + 1) * 16],
                              idn[:], is_transpose=True, start=(s == 0), stop=(s == 3)),
                     tick=(s == 3))
            utr = P.mark("tensor")
            P.wait("vector", "tensor", utr)
            P.op("vector", lambda e: e.tensor_copy(uTh[:], pb[2][0:16, 0:512]))
            usp = P.op("vector", lambda e: e.tensor_sub(uTl[:], pb[2][0:16, 0:512], uTh[:]))
            for s in range(4):
                bank = 4 + (s % 2)
                P.wait("tensor", "vector", usp if s < 2 else spev[s - 2])
                passes = p3(uTh[:, s * 128:(s + 1) * 128], uTl[:, s * 128:(s + 1) * 128],
                            w2h[:], w2l[:])
                v2v = mm_group(P, pb[bank][:, 0:512], passes, True, True)
                P.wait("vector", "tensor", v2v)
                addv = P.op("vector", lambda e, bank=bank: e.tensor_add(vscr, pb[bank][:, 0:512], bgk_bc[:]))
                P.wait("scalar", "vector", addv)
                av = P.op("scalar", lambda e: e.activation(sp32, vscr, AF.Exp, scale=-1.0))
                av = P.op("scalar", lambda e: e.activation(vscr, sp32, AF.Ln, bias=1.0))
                P.wait("vector", "scalar", av)
                P.op("vector", lambda e, s=s: e.tensor_copy(sph[:, s * 512:(s + 1) * 512], vscr))
                if s == 0:
                    spev = [0] * 4
                spev[s] = P.op("vector", lambda e, s=s:
                               e.tensor_tensor(spl[:, s * 512:(s + 1) * 512], vscr,
                                               sph[:, s * 512:(s + 1) * 512], ALU.subtract))
            prj_v = P.mark("vector")
            prj_a = P.mark("scalar")

            for s in range(4 if AT0_SCAN else 0):
                for hp in range(4):
                    P.wait("tensor", "scalar", P.mark("scalar"))
                    P.wait("tensor", "vector", P.mark("vector"))
                    for h01 in range(2):
                        hh = 2 * hp + h01
                        passes = [(sph[:, s * 512 + hh * 64: s * 512 + hh * 64 + 64], tmbd[:]),
                                  (spl[:, s * 512 + hh * 64: s * 512 + hh * 64 + 64], tmbd[:])]
                        mm_group(P, pb[0][64 * h01:64 * h01 + 64, 0:128], passes, True, True,
                                 tp=(0, 64 * h01), tick=(h01 == 1))
                    b2v = P.mark("tensor")
                    if AT0_LVL < 1:
                        continue
                    P.wait("scalar", "tensor", b2v)
                    P.op("scalar", lambda e: e.activation(eb[:], pb[0][:, 0:128], AF.Exp))
                    P.op("scalar", lambda e: e.activation(enb[:], pb[0][:, 0:128], AF.Exp, scale=-1.0))
                    P.wait("vector", "tensor", b2v)
                    P.op("vector", lambda e: e.tensor_copy(blc[:, 0:1], pb[0][:, 63:64]))
                    bv = P.op("vector", lambda e: e.tensor_copy(blc[:, 1:2], pb[0][:, 127:128]))
                    P.wait("scalar", "vector", bv)
                    for c01 in range(2):
                        P.op("scalar", lambda e, c01=c01:
                             e.activation(edc[:, c01 * 64:(c01 + 1) * 64],
                                          pb[0][:, c01 * 64:(c01 + 1) * 64], AF.Exp,
                                          scale=-1.0, bias=blc[:, c01:c01 + 1]))
                    ea = P.op("scalar", lambda e: e.activation(elc[:], blc[:], AF.Exp))
                    if AT0_LVL < 2:
                        continue
                    q_sl = qf[:, hp * 512 + s * 128: hp * 512 + (s + 1) * 128]
                    k_sl = kf[:, hp * 512 + s * 128: hp * 512 + (s + 1) * 128]
                    P.wait("vector", "scalar", ea)
                    P.op("vector", lambda e, q_sl=q_sl:
                         e.scalar_tensor_tensor(qe32[:], q_sl, 0.125, eb[:], ALU.mult, ALU.mult))
                    P.op("vector", lambda e: e.tensor_copy(qeh[:], qe32[:]))
                    P.op("vector", lambda e: e.tensor_sub(qel[:], qe32[:], qeh[:]))
                    P.op("vector", lambda e, k_sl=k_sl: e.tensor_mul(ke32[:], k_sl, enb[:]))
                    P.op("vector", lambda e: e.tensor_copy(keh[:], ke32[:]))
                    P.op("vector", lambda e: e.tensor_sub(kel[:], ke32[:], keh[:]))
                    kdv = P.op("vector", lambda e, k_sl=k_sl: e.tensor_mul(kd32[:], k_sl, edc[:]))
                    if AT0_LVL < 3:
                        continue
                    P.wait("tensor", "vector", kdv)
                    for h01 in range(2):
                        bk = 1 if h01 == 0 else 4
                        for c01 in range(2):
                            P.op("tensor", lambda e, c01=c01, h01=h01, bk=bk:
                                 e.matmul(pb[bk][64 * c01:64 * c01 + 64, 0:64],
                                          kd32[64 * h01:64 * h01 + 64, 64 * c01:64 * c01 + 64],
                                          eye64d[64 * h01:64 * h01 + 64, 0:64],
                                          start=True, stop=True,
                                          tile_position=(64 * h01, 64 * c01)),
                                 tick=(c01 == 1 and h01 == 1))
                    kdt = P.mark("tensor")
                    P.wait("vector", "tensor", kdt)
                    P.op("vector", lambda e: e.tensor_copy(kdph[:, 0:64], pb[1][:, 0:64]))
                    P.op("vector", lambda e: e.tensor_copy(kdph[:, 64:128], pb[4][:, 0:64]))
                    P.op("vector", lambda e: e.tensor_sub(kdpl[:, 0:64], pb[1][:, 0:64], kdph[:, 0:64]))
                    kde = P.op("vector", lambda e: e.tensor_sub(kdpl[:, 64:128], pb[4][:, 0:64], kdph[:, 64:128]))
                    if AT0_LVL < 4:
                        continue
                    for h01 in range(2):
                        bk = 2 if h01 == 0 else 6
                        for c01 in range(2):
                            sl_ = (slice(64 * h01, 64 * h01 + 64), slice(64 * c01, 64 * c01 + 64))
                            passes = p3(keh[sl_[0], sl_[1]], kel[sl_[0], sl_[1]],
                                        qeh[sl_[0], sl_[1]], qel[sl_[0], sl_[1]])
                            mm_group(P, pb[bk][64 * c01:64 * c01 + 64, 0:64],
                                     passes, first=True, last=True,
                                     tp=(64 * h01, 64 * c01),
                                     tick=(c01 == 1 and h01 == 1))
                    atv = P.mark("tensor")
                    P.wait("vector", "tensor", atv)
                    for h01 in range(2):
                        bk = 2 if h01 == 0 else 6
                        cs_ = slice(64 * h01, 64 * h01 + 64)
                        P.op("vector", lambda e, bk=bk, cs_=cs_:
                             e.tensor_mul(ath[:, cs_], pb[bk][:, 0:64], mask2w[:, 0:64]))
                        P.op("vector", lambda e, bk=bk, cs_=cs_:
                             e.tensor_mul(og32[:, 0:64], pb[bk][:, 0:64], mask2w[:, 0:64]))
                        P.op("vector", lambda e, bk=bk, cs_=cs_:
                             e.tensor_tensor(atl[:, cs_], og32[:, 0:64], ath[:, cs_], ALU.subtract))
                    ate = P.mark("vector")
                    if AT0_LVL < 5:
                        continue
                    for c01 in range(2):
                        gc = sl * 8 + s * 2 + c01
                        rs_ = slice(64 * c01, 64 * c01 + 64)
                        P.wait("tensor", "vector", P.mark("vector"))
                        o1p = []
                        for h01 in range(2):
                            hh = 2 * hp + h01
                            vsl = (rs_, slice(s * 1024 + hh * 128, s * 1024 + (hh + 1) * 128))
                            for lh, rh in p3(ath[rs_, 64 * h01:64 * h01 + 64],
                                             atl[rs_, 64 * h01:64 * h01 + 64],
                                             vth[vsl[0], vsl[1]], vtl[vsl[0], vsl[1]]):
                                o1p.append((pb[3][rs_, 128 * h01:128 * h01 + 128], lh, rh))
                        for i_, (reg, lh, rh) in enumerate(o1p):
                            P.op("tensor", lambda e, o=reg, a=lh, b=rh,
                                 st=(i_ == 0), sp=(i_ == len(o1p) - 1), c01=c01:
                                 e.matmul(o, a, b, start=st, stop=sp,
                                          tile_position=(64 * c01, 64 * c01)),
                                 tick=(i_ == len(o1p) - 1))
                        for h01 in range(2):
                            hh = 2 * hp + h01
                            ds_ = slice(64 * h01, 64 * h01 + 64)
                            if gc > 0:
                                s_sl = (ds_, slice(hp * 128, (hp + 1) * 128))
                                passes = p3(qeh[ds_, rs_], qel[ds_, rs_],
                                            s16h[s_sl[0], s_sl[1]], s16l[s_sl[0], s_sl[1]])
                                o2bk = 7 if h01 == 0 else 1
                                mm_group(P, pb[o2bk][rs_, 0:128], passes,
                                         True, True, tp=(64 * h01, 64 * c01), tick=(h01 == 1))
                        ov_ = P.mark("tensor")
                        if AT0_LVL < 6:
                            continue
                        for h01 in range(2):
                            passes = [(kdph[rs_, 64 * h01:64 * h01 + 64], vth[rs_, s * 1024 + (2 * hp + h01) * 128: s * 1024 + (2 * hp + h01 + 1) * 128]),
                                      (kdpl[rs_, 64 * h01:64 * h01 + 64], vth[rs_, s * 1024 + (2 * hp + h01) * 128: s * 1024 + (2 * hp + h01 + 1) * 128]),
                                      (kdph[rs_, 64 * h01:64 * h01 + 64], vtl[rs_, s * 1024 + (2 * hp + h01) * 128: s * 1024 + (2 * hp + h01 + 1) * 128])]
                            mm_group(P, pb[5][64 * h01:64 * h01 + 64, 0:128], passes, True, True,
                                     tp=(64 * c01, 64 * h01), tick=(h01 == 1))
                        sav = P.mark("tensor")
                        if AT0_LVL < 7:
                            continue
                        P.wait("vector", "tensor", ov_)
                        P.op("vector", lambda e, rs_=rs_:
                             e.tensor_copy(og32[rs_, :], pb[3][rs_, 0:256]))
                        if gc > 0:
                            P.op("vector", lambda e, rs_=rs_:
                                 e.tensor_add(og32[rs_, 0:128], og32[rs_, 0:128], pb[7][rs_, 0:128]))
                            P.op("vector", lambda e, rs_=rs_:
                                 e.tensor_add(og32[rs_, 128:256], og32[rs_, 128:256], pb[1][rs_, 0:128]))
                        osv = P.mark("vector")
                        P.wait("scalar", "vector", osv)
                        for h01 in range(2):
                            P.op("scalar", lambda e, rs_=rs_, h01=h01:
                                 e.activation(oscr[rs_, 128 * h01:128 * h01 + 128],
                                              og32[rs_, 128 * h01:128 * h01 + 128], AF.Square,
                                              accum_out=osq[rs_, h01:h01 + 1]))
                        sqa = P.op("scalar", lambda e, rs_=rs_:
                                   e.activation(osd[rs_, 0:2], osq[rs_, 0:2], AF.Sqrt,
                                                scale=1.0 / 128.0, bias=EPS_LN))
                        P.wait("vector", "scalar", sqa)
                        P.op("vector", lambda e, rs_=rs_: e.reciprocal(orc[rs_, 0:2], osd[rs_, 0:2]))
                        for h01 in range(2):
                            P.op("vector", lambda e, rs_=rs_, h01=h01:
                                 e.scalar_tensor_tensor(oscr[rs_, 128 * h01:128 * h01 + 128],
                                                        og32[rs_, 128 * h01:128 * h01 + 128],
                                                        orc[rs_, h01:h01 + 1],
                                                        nw_bc[rs_, :], ALU.mult, ALU.mult))
                        ogv = P.op("vector", lambda e, rs_=rs_, s=s, hp=hp:
                                   e.tensor_mul(og32[rs_, :], oscr[rs_, :],
                                                sg0[rs_, s * 1024 + hp * 256: s * 1024 + (hp + 1) * 256]))
                        P.wait("tensor", "vector", ogv)
                        for h01 in range(2):
                            P.op("tensor", lambda e, rs_=rs_, h01=h01, c01=c01:
                                 e.matmul(pb[4][:, 64 * h01:64 * h01 + 64],
                                          og32[rs_, 128 * h01:128 * h01 + 128],
                                          eye64d[rs_, 0:64],
                                          is_transpose=True, start=(h01 == 0), stop=(h01 == 1),
                                          tile_position=(64 * c01, 0)),
                                 tick=(h01 == 1))
                        otr = P.mark("tensor")
                        P.wait("vector", "tensor", otr)
                        ci = s * 2 + c01
                        for h01 in range(2):
                            hh = 2 * hp + h01
                            dcol = hh * 512 + ci * 64
                            P.op("vector", lambda e, h01=h01, dcol=dcol:
                                 e.tensor_copy(ogh[:, dcol:dcol + 64], pb[4][:, 64 * h01:64 * h01 + 64]))
                            P.op("vector", lambda e, h01=h01, dcol=dcol:
                                 e.tensor_tensor(ogl[:, dcol:dcol + 64], pb[4][:, 64 * h01:64 * h01 + 64],
                                                 ogh[:, dcol:dcol + 64], ALU.subtract))
                        P.wait("vector", "tensor", sav)
                        blk = slice(hp * 128, (hp + 1) * 128)
                        if gc == 0:
                            P.op("vector", lambda e, blk=blk: e.tensor_copy(s32[:, blk], pb[5][:, 0:128]))
                        else:
                            P.op("vector", lambda e, blk=blk, c01=c01:
                                 e.tensor_scalar(s32[:, blk], s32[:, blk], elc[:, c01:c01 + 1],
                                                 None, ALU.mult))
                            P.op("vector", lambda e, blk=blk:
                                 e.tensor_add(s32[:, blk], s32[:, blk], pb[5][:, 0:128]))
                        P.op("vector", lambda e, blk=blk: e.tensor_copy(s16h[:, blk], s32[:, blk]))
                        P.op("vector", lambda e, blk=blk:
                             e.tensor_tensor(s16l[:, blk], s32[:, blk], s16h[:, blk], ALU.subtract))
            if sl == 0 and DBG:
                P.wait("gpsimd", "vector", P.mark("vector"))
                P.dma(st_out, dbgx[:, 0:2048], sph[:])
                P.dma(st_out, dbgx[:, 2048:4096], spl[:])
                P.dma(st_out, dbgq, ogh[:])
                P.dma(st_out, dbg[:, 0:512], s32[:])
                P.dma(st_out, dbg[:, 512:2560], qf[:])
                P.dma(st_out, dbg[:, 2560:2624], u_sb[:])
                P.dma(st_out, dbg[0:16, 2624:3136], uTh[:])
                dv_ = P.dma(st_out, dbg[:, 3136:3648], bgk_bc[:])
                P.dwait("vector", st_out, dv_)
                P.dwait("tensor", st_out, dv_)
            og_v = P.mark("vector")
            pj = [0, 0]
            fin = [0, 0]
            for idx in range(8):
                s, n = idx // 2, idx % 2
                buf = idx % 2
                wv = stream_nm("a0_wo", buf, n * 512, 512, pj[idx % 2] if idx >= 2 else P.mark("tensor"))
                bank = 6 + (idx % 2)
                P.dwait("tensor", st_w[buf], wv)
                P.wait("tensor", "vector", og_v if idx < 2 else fin[idx % 2])
                passes = []
                for kc in range(8):
                    passes += p3(ogh[:, kc * 512 + s * 128: kc * 512 + (s + 1) * 128],
                                 ogl[:, kc * 512 + s * 128: kc * 512 + (s + 1) * 128],
                                 wbh[buf][:, kc * 512:(kc + 1) * 512],
                                 wbl[buf][:, kc * 512:(kc + 1) * 512])
                pj[idx % 2] = mm_group(P, pb[bank][:, 0:512], passes, True, True)
                P.wait("vector", "tensor", pj[idx % 2])
                ts = 4 * sl + s
                fin[idx % 2] = P.op("vector", lambda e, bank=bank, ts=ts, n=n:
                                    e.tensor_tensor(x_res[:, ts * D + n * 512: ts * D + (n + 1) * 512],
                                                    pb[bank][:, 0:512],
                                                    x_res[:, ts * D + n * 512: ts * D + (n + 1) * 512],
                                                    ALU.add))
        P.emit()
        ph.close()

    def emit_ffn1(tag):
        ph = contextlib.ExitStack()
        sbt = lambda n, s, d: ph.enter_context(nc.sbuf_tensor(tag + n, s, d))
        wdb = [sbt(f"wd{i}", [128, 1024], BF16) for i in range(2)]
        hf_h = sbt("hfh", [128, 24 * 512], BF16)
        hf_l = sbt("hfl", [128, 24 * 512], BF16)
        sg_c = sbt("sgc", [128, 512], F32)
        h32 = sbt("h32", [128, 512], F32)
        st_wd = [P.dstream(tag + f"wd{i}", "sync") for i in range(2)]

        for t in range(4):
            for s in range(4):
                ln_tile(4 * t + s, 6, 7)
            xv = P.mark("vector")
            gmm = [0, 0]
            umm = [0, 0]
            sil = [0, 0]
            hsp = [0, 0]
            for c in range(24):
                buf = c % 2
                P.wait("sync", "tensor", umm[buf] if c >= 2 else P._ln_tr)
                src_g = scr["f1_wg"].rearrange("(kc p) m -> p kc m", p=128)[:, :, c * 128:(c + 1) * 128]
                src_u = scr["f1_wu"].rearrange("(kc p) m -> p kc m", p=128)[:, :, c * 128:(c + 1) * 128]
                P.dma(st_w[buf], wbh[buf][:, 0:1024].rearrange("p (kc m) -> p kc m", kc=8), src_g)
                wv = P.dma(st_w[buf], wbl[buf][:, 0:1024].rearrange("p (kc m) -> p kc m", kc=8), src_u)
                gb = 0 + buf
                ub = 2 + buf
                P.dwait("tensor", st_w[buf], wv)
                if c < 2:
                    P.wait("tensor", "vector", xv)
                P.wait("tensor", "scalar", sil[buf])
                passes = []
                for kc in range(8):
                    wg_ = wbh[buf][:, kc * 128:(kc + 1) * 128]
                    passes += [(wg_, xlt_h[:, kc * 512:(kc + 1) * 512]),
                               (wg_, xlt_l[:, kc * 512:(kc + 1) * 512])]
                gmm[buf] = mm_group(P, pb[gb][:, 0:512], passes, True, True)
                P.wait("tensor", "vector", hsp[buf])
                passes = []
                for kc in range(8):
                    wu_ = wbl[buf][:, kc * 128:(kc + 1) * 128]
                    passes += [(wu_, xlt_h[:, kc * 512:(kc + 1) * 512]),
                               (wu_, xlt_l[:, kc * 512:(kc + 1) * 512])]
                umm[buf] = mm_group(P, pb[ub][:, 0:512], passes, True, True)
                P.wait("scalar", "tensor", gmm[buf])
                sil[buf] = P.op("scalar", lambda e, gb=gb:
                                e.activation(sg_c[:], pb[gb][:, 0:512], AF.Silu,
                                             scale=1.0 / 127.0))
                P.wait("vector", "scalar", sil[buf])
                P.wait("vector", "tensor", umm[buf])
                P.op("vector", lambda e, ub=ub: e.tensor_mul(h32[:], sg_c[:], pb[ub][:, 0:512]))
                P.op("vector", lambda e, c=c: e.tensor_copy(hf_h[:, c * 512:(c + 1) * 512], h32[:]))
                hsp[buf] = P.op("vector", lambda e, c=c:
                                e.tensor_tensor(hf_l[:, c * 512:(c + 1) * 512], h32[:],
                                                hf_h[:, c * 512:(c + 1) * 512], ALU.subtract))
            hv = P.mark("vector")
            if FFN1_LVL < 2:
                continue
            P.wait("tensor", "vector", hv)
            wdm = [0, 0]
            for c in range(24):
                buf = c % 2
                P.wait("sync", "tensor", wdm[buf] if c >= 2 else P._ln_tr)
                wv = P.dma(st_wd[buf], wdb[buf][:], scr["f1_wd"][c * 128:(c + 1) * 128, :])
                P.dwait("tensor", st_wd[buf], wv)
                for idx in range(8):
                    s, n = idx // 2, idx % 2
                    hh_ = hf_h[:, c * 512 + s * 128: c * 512 + (s + 1) * 128]
                    hl_ = hf_l[:, c * 512 + s * 128: c * 512 + (s + 1) * 128]
                    wd_ = wdb[buf][:, n * 512:(n + 1) * 512]
                    st_ = (c == 0)
                    sp_ = (c == 23)
                    P.op("tensor", lambda e, o=pb[idx][:, 0:512], a=hh_, b=wd_, st=st_:
                         e.matmul(o, a, b, start=st, stop=False), tick=False)
                    wdm[buf] = P.op("tensor", lambda e, o=pb[idx][:, 0:512], a=hl_, b=wd_, sp=sp_:
                                    e.matmul(o, a, b, start=False, stop=sp), tick=(idx == 7))
            fmm = P.mark("tensor")
            P.wait("vector", "tensor", fmm)
            for idx in range(8):
                s, n = idx // 2, idx % 2
                ts = 4 * t + s
                P.op("vector", lambda e, idx=idx, ts=ts, n=n:
                     e.scalar_tensor_tensor(
                         x_res[:, ts * D + n * 512: ts * D + (n + 1) * 512],
                         pb[idx][:, 0:512], 0.5 / (127.0 * 127.0),
                         x_res[:, ts * D + n * 512: ts * D + (n + 1) * 512],
                         ALU.mult, ALU.add))
        P.emit()
        ph.close()

    def emit_att1():
        ph = contextlib.ExitStack()
        sbt = lambda n, s, d: ph.enter_context(nc.sbuf_tensor("at1" + n, s, d))
        q1 = sbt("q1", [128, 4 * 1024], F32)
        k1 = sbt("k1", [128, 4 * 1024], F32)
        v32 = sbt("v32", [128, 4 * 1024], F32)
        sg1 = sbt("sg1", [128, 4 * 1024], F32)
        qk = sbt("qk", [128, 16 * 16], F32)
        nrm = sbt("nrm", [128, 16], F32)
        ksb = sbt("ksb", [128, D], F32)
        gam_bc = sbt("gam", [128, D], F32)
        bet_bc = sbt("bet", [128, D], F32)
        scr1 = sbt("scr1", [128, 512], F32)
        scr2 = sbt("scr2", [128, 512], F32)
        zscr = sbt("zscr", [128, 1024], F32)
        z2 = sbt("z2", [128, 1024], F32)

        build_bcast(a1["gam"], gam_bc, D)
        build_bcast(a1["bet"], bet_bc, D)

        def proj_elu(nm, dst, sl):
            pj = [0] * 8
            pev = [("vector", 0)] * 8
            for idx in range(8):
                s, n = idx // 2, idx % 2
                buf = idx % 2
                wv = stream_nm(nm, buf, n * 512, 512, pj[idx - 2] if idx >= 2 else P.mark("tensor"))
                bank = 4 + (idx % 2)
                P.dwait("tensor", st_w[buf], wv)
                if idx >= 2:
                    P.wait("tensor", pev[idx - 2][0], pev[idx - 2][1])
                passes = []
                for kc in range(8):
                    passes += p3(xlt_h[:, kc * 512 + s * 128: kc * 512 + (s + 1) * 128],
                                 xlt_l[:, kc * 512 + s * 128: kc * 512 + (s + 1) * 128],
                                 wbh[buf][:, kc * 512:(kc + 1) * 512],
                                 wbl[buf][:, kc * 512:(kc + 1) * 512])
                pj[idx] = mm_group(P, pb[bank][:, 0:512], passes, True, True)
                col = s * 1024 + n * 512
                if dst is v32:
                    P.wait("vector", "tensor", pj[idx])
                    pev[idx] = ("vector", P.op("vector", lambda e, bank=bank, col=col:
                                e.tensor_copy(v32[:, col:col + 512], pb[bank][:, 0:512])))
                elif dst is sg1:
                    P.wait("scalar", "tensor", pj[idx])
                    pev[idx] = ("scalar", P.op("scalar", lambda e, bank=bank, col=col:
                                e.activation(sg1[:, col:col + 512], pb[bank][:, 0:512], AF.Silu)))
                else:
                    P.wait("scalar", "tensor", pj[idx])
                    a_ = P.op("scalar", lambda e, bank=bank:
                              e.activation(scr1[:], pb[bank][:, 0:512], AF.Relu))
                    P.wait("vector", "scalar", a_)
                    v_ = P.op("vector", lambda e, bank=bank:
                              e.tensor_sub(scr2[:], pb[bank][:, 0:512], scr1[:]))
                    P.wait("scalar", "vector", v_)
                    a_ = P.op("scalar", lambda e: e.activation(zscr[:, 0:512], scr2[:], AF.Exp))
                    P.wait("vector", "scalar", a_)
                    pev[idx] = ("vector", P.op("vector", lambda e, dst=dst, col=col:
                                e.tensor_add(dst[:, col:col + 512], zscr[:, 0:512], scr1[:])))
            return P.mark("vector")

        nks = 0
        for sl in range(NSL):
            for s in range(4):
                ln_tile(4 * sl + s, 6, 7)
            kv_ = proj_elu("a1_wk", k1, sl)
            P.wait("tensor", "vector", kv_)
            for s in range(4):
                for n in range(2):
                    nks += 1
                    P.op("tensor", lambda e, s=s, n=n, first=(sl == 0 and s == 0),
                         last=(sl == NSL - 1 and s == 3):
                         e.matmul(pb[n][0:1, 0:512],
                                  onesc[:], k1[:, s * 1024 + n * 512: s * 1024 + (n + 1) * 512],
                                  start=first, stop=last),
                         tick=(s == 3 and n == 1))
        ksmm = P.mark("tensor")
        P.wait("vector", "tensor", ksmm)
        P.op("vector", lambda e: e.tensor_copy(scr1[0:1, 0:512], pb[0][0:1, 0:512]))
        ksv = P.op("vector", lambda e: e.tensor_copy(scr2[0:1, 0:512], pb[1][0:1, 0:512]))
        P.wait("tensor", "vector", ksv)
        P.op("tensor", lambda e: e.matmul(pb[2][:, 0:512], ones1[:], scr1[0:1, 0:512], start=True, stop=True))
        kbv = P.op("tensor", lambda e: e.matmul(pb[3][:, 0:512], ones1[:], scr2[0:1, 0:512], start=True, stop=True))
        P.wait("vector", "tensor", kbv)
        P.op("vector", lambda e: e.tensor_copy(ksb[:, 0:512], pb[2][:, 0:512]))
        P.op("vector", lambda e: e.tensor_copy(ksb[:, 512:1024], pb[3][:, 0:512]))

        for sl in range(NSL):
            for s in range(4):
                ln_tile(4 * sl + s, 6, 7)
            qv_ = proj_elu("a1_wq", q1, sl)
            kv_ = proj_elu("a1_wk", k1, sl)
            vv_ = proj_elu("a1_wv", v32, sl)
            gv_ = proj_elu("a1_wg", sg1, sl)
            wo_f = [0, 0]
            for s in range(4):
                c0 = s * 1024
                P.op("vector", lambda e, c0=c0: e.tensor_mul(zscr[:], q1[:, c0:c0 + 1024], k1[:, c0:c0 + 1024]), tick=False)
                P.op("vector", lambda e, s=s: e.tensor_reduce(
                    qk[:, s * 16:(s + 1) * 16].rearrange("p g -> p g ()"),
                    zscr[:].rearrange("p (g k) -> p g k", g=16), AX.X, ALU.add), tick=False)
                P.op("vector", lambda e, c0=c0: e.tensor_mul(zscr[:], q1[:, c0:c0 + 1024], ksb[:]), tick=False)
                v_ = P.op("vector", lambda e: e.tensor_reduce(
                    nrm[:].rearrange("p g -> p g ()"),
                    zscr[:].rearrange("p (g k) -> p g k", g=16), AX.X, ALU.add))
                P.wait("vector", "vector", v_)
                P.op("vector", lambda e: e.tensor_single_scalar(nrm[:], nrm[:], 1e-6, ALU.add), tick=False)
                v_ = P.op("vector", lambda e: e.reciprocal(nrm[:], nrm[:]))
                P.wait("vector", "vector", v_)
                v_ = P.op("vector", lambda e, s=s: e.tensor_mul(nrm[:], nrm[:], qk[:, s * 16:(s + 1) * 16]))
                P.wait("vector", "vector", v_)
                for h in range(16):
                    P.op("vector", lambda e, c0=c0, h=h:
                         e.tensor_scalar(zscr[:, h * 64:(h + 1) * 64],
                                         v32[:, c0 + h * 64: c0 + (h + 1) * 64],
                                         nrm[:, h:h + 1], None, ALU.mult),
                         tick=(h == 15))
                sv = P.op("vector", lambda e: e.tensor_reduce(ln_s1[:], zscr[:], AX.X, ALU.add))
                P.wait("scalar", "vector", sv)
                a_ = P.op("scalar", lambda e: e.activation(z2[:], zscr[:], AF.Square, accum_out=ln_ssq[:]))
                P.wait("vector", "vector", sv)
                v_ = P.op("vector", lambda e: e.tensor_single_scalar(ln_mu[:], ln_s1[:], 1.0 / D, ALU.mult))
                P.wait("vector", "scalar", a_)
                P.op("vector", lambda e: e.tensor_single_scalar(ln_va[:], ln_ssq[:], 1.0 / D, ALU.mult), tick=False)
                P.wait("vector", "vector", v_)
                P.op("vector", lambda e: e.tensor_mul(ln_m2[:], ln_mu[:], ln_mu[:]), tick=False)
                v_ = P.op("vector", lambda e: e.tensor_sub(ln_va[:], ln_va[:], ln_m2[:]))
                P.wait("scalar", "vector", v_)
                a_ = P.op("scalar", lambda e: e.activation(ln_sd[:], ln_va[:], AF.Sqrt, bias=EPS_LN))
                P.wait("vector", "scalar", a_)
                v_ = P.op("vector", lambda e: e.reciprocal(ln_rs[:], ln_sd[:]))
                P.wait("vector", "vector", v_)
                v_ = P.op("vector", lambda e:
                          e.tensor_scalar(z2[:], zscr[:], ln_mu[:], ln_rs[:], ALU.subtract, ALU.mult))
                P.wait("vector", "vector", v_)
                P.op("vector", lambda e: e.tensor_mul(z2[:], z2[:], gam_bc[:]), tick=False)
                P.op("vector", lambda e: e.tensor_add(z2[:], z2[:], bet_bc[:]), tick=False)
                zv = P.op("vector", lambda e, c0=c0: e.tensor_mul(z2[:], z2[:], sg1[:, c0:c0 + 1024]))
                _transpose_split(z2, xlt_h, xlt_l, s, 6, 7, zv)
                for n in range(2):
                    buf = n % 2
                    wv = stream_nm("a1_wo", buf, n * 512, 512, P.mark("tensor"))
                    bank = 2 + n
                    P.dwait("tensor", st_w[buf], wv)
                    P.wait("tensor", "vector", P.mark("vector"))
                    passes = []
                    for kc in range(8):
                        passes += p3(xlt_h[:, kc * 512 + s * 128: kc * 512 + (s + 1) * 128],
                                     xlt_l[:, kc * 512 + s * 128: kc * 512 + (s + 1) * 128],
                                     wbh[buf][:, kc * 512:(kc + 1) * 512],
                                     wbl[buf][:, kc * 512:(kc + 1) * 512])
                    wmm = mm_group(P, pb[bank][:, 0:512], passes, True, True)
                    P.wait("vector", "tensor", wmm)
                    ts = 4 * sl + s
                    P.op("vector", lambda e, bank=bank, ts=ts, n=n:
                         e.tensor_tensor(x_res[:, ts * D + n * 512: ts * D + (n + 1) * 512],
                                         pb[bank][:, 0:512],
                                         x_res[:, ts * D + n * 512: ts * D + (n + 1) * 512],
                                         ALU.add))
        P.emit()
        ph.close()

    def emit_store():
        ov = P.dma(st_out, out.rearrange("(n p) f -> p n f", p=128),
                   x_res[:].rearrange("p (n f) -> p n f", n=NT))
        P.dwait("gpsimd", st_out, ov)
        P.emit()

    for name in ACTIVE:
        if name in ("ffn0a", "ffn0b"):
            emit_ffn0(name)
        elif name == "att0":
            emit_att0()
        elif name in ("ffn1a", "ffn1b"):
            emit_ffn1(name)
        elif name == "att1":
            emit_att1()
    emit_store()
    es.close()
    return nc


def host_inputs(inputs):
    import ml_dtypes
    C = lambda a: np.ascontiguousarray(np.asarray(a, dtype=np.float32))
    CI = lambda a: np.ascontiguousarray(np.asarray(a, dtype=np.int32))
    TT = lambda a: np.ascontiguousarray(np.asarray(a, dtype=np.float32).T)

    kT = np.asarray(inputs["L0_ffn_k_ffwd"], np.float32).reshape(HF, NE * SE, DF)
    kT = np.ascontiguousarray(kT.transpose(0, 2, 1))
    vi = np.asarray(inputs["L0_ffn_v_ffwd"], np.float32).reshape(HF, NE * SE // 128, 128, DF)
    vi = np.ascontiguousarray(vi.transpose(0, 2, 1, 3).reshape(HF, 128, NE * SE))

    tm = np.zeros((128, 128), np.float32)
    for blk in range(2):
        for a_ in range(64):
            tm[blk * 64 + a_, blk * 64 + a_: blk * 64 + 64] = -1.0 / GATE_NORM
    msk = np.tile(np.triu(np.ones((64, 64), np.float32)), (2, 2))
    base = {
        "a0_wq": C(inputs["L0_attn_Wq"]), "a0_wk": C(inputs["L0_attn_Wk"]),
        "a0_wv": C(inputs["L0_attn_Wv"]), "a0_wg": C(inputs["L0_attn_Wg"]),
        "a0_wgk1": C(inputs["L0_attn_Wgk1"]), "a0_wgk2": C(inputs["L0_attn_Wgk2"]),
        "a0_wo": C(inputs["L0_attn_Wo"]),
        "a0_bgk2": C(inputs["L0_attn_bgk2"]).reshape(1, 512),
        "a0_nw": C(inputs["L0_attn_norm_w"]).reshape(1, 128),
        "f0_rt": TT(inputs["L0_ffn_w_router"]), "f0_mh": TT(inputs["L0_ffn_w_mh"]),
        "f0_mg": TT(inputs["L0_ffn_w_mg"]), "f0_kT": kT, "f0_vi": vi,
        "a1_wq": C(inputs["L1_attn_Wq"]), "a1_wk": C(inputs["L1_attn_Wk"]),
        "a1_wv": C(inputs["L1_attn_Wv"]), "a1_wg": C(inputs["L1_attn_Wg"]),
        "a1_wo": C(inputs["L1_attn_Wo"]),
        "a1_gam": C(inputs["L1_attn_gamma"]).reshape(1, D),
        "a1_bet": C(inputs["L1_attn_beta"]).reshape(1, D),
        "f1_wg": CI(np.asarray(inputs["L1_ffn_w_gate"]).T),
        "f1_wu": CI(np.asarray(inputs["L1_ffn_w_up"]).T),
        "f1_wd": CI(np.asarray(inputs["L1_ffn_w_down"]).T),
        "c_ident": np.eye(128, dtype=np.float32),
        "c_tmbd": tm.astype(ml_dtypes.bfloat16),
        "c_mask2": msk,
        "c_ones1": np.ones((1, 128), np.float32),
        "c_onesc": np.ones((128, 1), np.float32),
        "c_eye64d": np.vstack([np.eye(64, dtype=np.float32)] * 2),
        "c_cols": np.tile(np.asarray(CONST_COLS, np.float32)[None, :], (128, 1)),
    }
    return base


def kernel(**inputs):
    xs = np.ascontiguousarray(np.asarray(inputs["L0_attn_x"], dtype=np.float32))
    base = host_inputs(inputs)
    nc = build_nc()
    in_maps = []
    for b in range(8):
        m = dict(base)
        m["x_in"] = np.ascontiguousarray(xs[b])
        in_maps.append(m)
    res = run_bass_kernel_spmd(nc, in_maps, core_ids=list(range(8)))
    return np.stack([res.results[b]["out"] for b in range(8)], axis=0)
